# version 01 (final)
# baseline (speedup 1.0000x reference)
import contextlib
import numpy as np
import ml_dtypes
import concourse.bass as bass
import concourse.mybir as mybir
from concourse.bass_utils import run_bass_kernel_spmd

F32 = mybir.dt.float32
BF16 = mybir.dt.bfloat16
I32 = mybir.dt.int32
AF = mybir.ActivationFunctionType
ALU = mybir.AluOpType

NCORES = 8
TPC = 1024
D = 4096
NIT = 32
NPAN = 32
PAN = 128
EPS = 1e-5
ATT_SCALE = float(1.0 / np.sqrt(128.0))
Q_SCALE = float(1.0 / 4096.0 / np.sqrt(128.0))
MAGIC = 1_700_001
ANTI = 900_017
MAXIT = 2_000_000

DBG = False
RACE_DETECT = True

N_EVQKV = 6
N_EVTR = 12
N_EVHT = 20



class DStream:

    def __init__(self, mksem, name, n=2):
        self.sems = [mksem(f"{name}{j}") for j in range(n)]
        self.n = n
        self.tot = [0] * n
        self.targets = []

    def batch(self, eng, dmas):
        b = len(self.targets)
        j = b % self.n
        if self.tot[j] > 0:
            eng.wait_ge(self.sems[j], self.tot[j])
        for out, in_ in dmas:
            eng.dma_start(out=out, in_=in_).then_inc(self.sems[j], 16)
            self.tot[j] += 16
        self.targets.append((self.sems[j], self.tot[j]))

    def ready(self, b):
        return self.targets[b]


def _build():
    nc = bass.Bass("TRN2", target_bir_lowering=False, detect_race_conditions=RACE_DETECT)
    ap = lambda name, shape, dt, kind: nc.dram_tensor(name, shape, dt, kind=kind).ap()

    xT_own = ap("xT_own", [D, TPC], F32, "ExternalInput")
    xT_red = ap("xT_red", [D, TPC], F32, "ExternalInput")
    x_pan_in = ap("x_pan_in", [NPAN, TPC, PAN], F32, "ExternalInput")
    wq_in = ap("wq_in", [D, 4096], F32, "ExternalInput")
    wk_in = ap("wk_in", [D, 128], F32, "ExternalInput")
    wv_in = ap("wv_in", [D, 128], F32, "ExternalInput")
    bq_in = ap("bq_in", [128, 32], F32, "ExternalInput")
    bk_in = ap("bk_in", [128, 1], F32, "ExternalInput")
    bv_in = ap("bv_in", [128, 1], F32, "ExternalInput")
    w_pan_in = ap("w_pan_in", [NPAN, D, PAN], F32, "ExternalInput")
    down_in = ap("down_in", [D, 512], F32, "ExternalInput")
    up_pan_in = ap("up_pan_in", [NPAN, 512, PAN], F32, "ExternalInput")
    wqf_in = ap("wqf_in", [D, 128], F32, "ExternalInput")
    keys_in = ap("keys_in", [128, 8], F32, "ExternalInput")
    ident_in = ap("ident_in", [128, 128], F32, "ExternalInput")
    ones_in = ap("ones_in", [128, 128], F32, "ExternalInput")
    onesb_in = ap("onesb_in", [128, 1], BF16, "ExternalInput")
    meta_in = ap("meta_in", [1, 8], I32, "ExternalInput")

    out_pan = ap("out_pan", [NPAN, TPC, PAN], F32, "ExternalOutput")
    if DBG:
        dbg_m = ap("dbg_m", [1, TPC], F32, "ExternalOutput")
        dbg_r = ap("dbg_r", [1, TPC], F32, "ExternalOutput")
        dbg_xn = ap("dbg_xn", [128, TPC], BF16, "ExternalOutput")
        dbg_mv = ap("dbg_mv", [128, 72], F32, "ExternalOutput")
        dbg_akv = ap("dbg_akv", [128, 3 * TPC], F32, "ExternalOutput")
        dbg_o = ap("dbg_o", [128, 8 * 128], F32, "ExternalOutput")
        dbg_q = ap("dbg_q", [128, 4], F32, "ExternalOutput")
        dbg_c = ap("dbg_c", [1, 16], F32, "ExternalOutput")
        dbg_cc = ap("dbg_cc", [128, 4], F32, "ExternalOutput")
        dbg_hsT = ap("dbg_hsT", [128, 4 * TPC], BF16, "ExternalOutput")
        dbg_cap1 = ap("dbg_cap1", [1, TPC], F32, "ExternalOutput")
        dbg_cap2 = ap("dbg_cap2", [128, PAN], F32, "ExternalOutput")
        dbg_cap3 = ap("dbg_cap3", [128, 4], F32, "ExternalOutput")
        dbg_sc = ap("dbg_sc", [128, 2048], F32, "ExternalOutput")
        dbg_e = ap("dbg_e", [128, 2048], BF16, "ExternalOutput")
        dbg_f = ap("dbg_f", [128, 2048], BF16, "ExternalOutput")
        dbg_dn = ap("dbg_dn", [128, 48], F32, "ExternalOutput")
        dbg_mvb = ap("dbg_mvb", [128, 64], BF16, "ExternalOutput")
        dbg_r31 = ap("dbg_r31", [128, 1024], BF16, "ExternalOutput")
        dbg_dc = ap("dbg_dc", [128, 8], F32, "ExternalOutput")

    mail = nc.dram_tensor("mail", [128, 8], F32, addr_space="Shared")
    flags = nc.dram_tensor("flags", [1, 8], I32, addr_space="Shared")
    coeff_dram = nc.dram_tensor("coeff_dram", [1, 8], F32)

    with contextlib.ExitStack() as st:
        S_ = lambda name, shape, dt: st.enter_context(nc.sbuf_tensor(name, shape, dt))
        P_ = lambda name: st.enter_context(nc.psum_tensor(name, [128, 512], F32))
        SEM = lambda name: st.enter_context(nc.semaphore(name))

        ring = S_("ring", [128, 33 * TPC], BF16)
        regB = S_("regB", [128, 16384], F32)
        sqb = [S_("sq0", [128, TPC], BF16), S_("sq1", [128, TPC], BF16)]
        t1f = S_("t1f", [128, TPC], F32)
        wqsum = S_("wqsum", [128, NIT * 128], BF16)
        wqstg = S_("wqstg", [128, 2 * 4096], BF16)
        wqscr = S_("wqscr", [128, 3072], BF16)
        wk_sb = S_("wk_sb", [128, NIT * 128], BF16)
        wv_sb = S_("wv_sb", [128, NIT * 128], BF16)
        wqf_sb = S_("wqf_sb", [128, NIT * 128], BF16)
        statp = S_("statp", [128, TPC], F32)
        statq = S_("statq", [128, TPC], F32)
        keys_sb = S_("keys_sb", [128, 8], F32)
        ident_sb = S_("ident_sb", [128, 128], F32)
        ones_sb = S_("ones_sb", [128, 128], F32)
        onesb_sb = S_("onesb_sb", [128, 1], BF16)
        bq_sb = S_("bq_sb", [128, 32], F32)
        bqs = S_("bqs", [128, 4], F32)
        mv_sb = S_("mv_sb", [128, 72], F32)
        mvb_sb = S_("mvb_sb", [128, 64], BF16)
        q_sb = S_("q_sb", [128, 4], F32)
        coef_sb = S_("coef_sb", [1, 16], F32)
        ccol_sb = S_("ccol_sb", [128, 4], F32)
        o_sb = S_("o_sb", [128, 8 * 128], F32)
        g_sb = S_("g_sb", [128, 2 * PAN], F32)
        flf = S_("flf", [1, 8], I32)
        meta_sb = S_("meta_sb", [1, 8], I32)
        qex_sb = S_("qex_sb", [128, 1], F32)
        dcap_sb = S_("dcap_sb", [128, 8], F32) if DBG else None

        wbuf = wqstg
        ubuf = wqscr
        dbuf8 = wqf_sb

        red_t = lambda r: regB[:, 512 * r:512 * (r + 1)].bitcast(BF16)
        sc_v = regB[:, 0:2048]
        e_v = regB[:, 2048:3072].bitcast(BF16)
        f_v = regB[:, 3072:4096].bitcast(BF16)
        trA = regB[:, 4096:4608].bitcast(BF16)
        trB = regB[:, 4608:5120].bitcast(BF16)
        trC = regB[:, 5120:5376].bitcast(BF16)
        trD = regB[:, 5376:5632].bitcast(BF16)
        a_sb = regB[:, 5888:6912]
        k_sb = regB[:, 6912:7936]
        v_sb = regB[:, 7936:8960]
        x_stg = [regB[:, 8960:9984], regB[:, 9984:11008]]
        o_stg = [regB[:, 11008:12032], regB[:, 12032:13056]]
        v_bf = regB[:, 13056:13568].bitcast(BF16)
        t2f_all = regB[:, 13568:14592]
        o_den = regB[:, 14592:15104].bitcast(BF16)
        o_num = regB[:, 15104:15616].bitcast(BF16)

        ps = [P_(f"ps{i}") for i in range(8)]

        stX = DStream(SEM, "stX", 2)
        stWQ = DStream(SEM, "stWQ", 2)
        stDN = DStream(SEM, "stDN", 2)
        stPN = DStream(SEM, "stPN", 2)
        sWS = SEM("sWS")
        sSq, sTst, sBC, sFin, sSf = SEM("sSq"), SEM("sTst"), SEM("sBC"), SEM("sFin"), SEM("sSf")
        sNrm, sWQr, sQKV, sEvV, sTr = SEM("sNrm"), SEM("sWQr"), SEM("sQKV"), SEM("sEvV"), SEM("sTr")
        sScV, sAtS, sAtV = SEM("sScV"), SEM("sAtS"), SEM("sAtV")
        sHT, sBase, sSil, sFu, sOut = SEM("sHT"), SEM("sBase"), SEM("sSil"), SEM("sFu"), SEM("sOut")
        sMail, sCc, sQas = SEM("sMail"), SEM("sCc"), SEM("sQas")
        sG = SEM("sG")
        dsem, psem = SEM("dsem"), SEM("psem")

        block = st.enter_context(nc.Block())

        xn_slot = lambda i: (i + 32) % 33
        ring_x = lambda i, c0, c1: ring[:, i * TPC + c0:i * TPC + c1]
        ring_xn = lambda i, c0, c1: ring[:, xn_slot(i) * TPC + c0:xn_slot(i) * TPC + c1]

        NSMALL = 11
        T_WS = NSMALL * 16

        @block.gpsimd
        def _(g):
            ws_ctr = [0]
            def small(dst, src):
                if ws_ctr[0]:
                    g.wait_ge(sWS, ws_ctr[0])
                g.dma_start(out=dst, in_=src).then_inc(sWS, 16)
                ws_ctr[0] += 16
            small(meta_sb[:], meta_in[:])
            small(ident_sb[:], ident_in[:])
            small(ones_sb[:], ones_in[:])
            small(onesb_sb[:], onesb_in[:])
            small(keys_sb[:], keys_in[:])
            small(bq_sb[:], bq_in[:])
            small(bqs[:, 1:2], bk_in[:])
            small(bqs[:, 2:3], bv_in[:])
            small(wk_sb[:].rearrange("p (n e) -> p n e", e=128),
                  wk_in.rearrange("(n p) e -> p n e", p=128))
            small(wv_sb[:].rearrange("p (n e) -> p n e", e=128),
                  wv_in.rearrange("(n p) e -> p n e", p=128))
            small(wqf_sb[:].rearrange("p (n e) -> p n e", e=128),
                  wqf_in.rearrange("(n p) e -> p n e", p=128))
            for b in range(8):
                stX.batch(g, [(
                    ring[:, 4 * b * TPC:(4 * b + 4) * TPC].rearrange("p (n t) -> p n t", t=TPC),
                    xT_own.rearrange("(n p) t -> p n t", p=128)[:, 4 * b:4 * b + 4, :])])
            for b in range(8):
                stX.batch(g, [(
                    regB[:, 4 * b * 512:(4 * b + 4) * 512].bitcast(BF16).rearrange("p (n t) -> p n t", t=TPC),
                    xT_red.rearrange("(n p) t -> p n t", p=128)[:, 4 * b:4 * b + 4, :])])
            for i in range(NIT):
                if i >= 2:
                    g.wait_ge(sWQr, i - 1)
                stWQ.batch(g, [(wqstg[:, 4096 * (i % 2):4096 * (i % 2 + 1)],
                                wq_in[128 * i:128 * (i + 1), :])])
            g.wait_ge(sWQr, NIT)
            g.wait_ge(sQKV, 7)
            for b in range(8):
                if b >= 2:
                    g.wait_ge(sHT, 4 * (b - 1))
                stDN.batch(g, [(
                    dbuf8[:, 512 * ((4 * b) % 8):512 * ((4 * b) % 8) + 2048
                          ].rearrange("p (n e) -> p n e", e=512),
                    down_in.rearrange("(n p) e -> p n e", p=128)[:, 4 * b:4 * b + 4, :])])
            g.wait_ge(sNrm, 64)
            for p in range(NPAN):
                if p >= 2:
                    g.wait_ge(sBase, 8 * (p - 1))
                    g.wait_ge(sFu, 8 * (p - 1))
                stPN.batch(g, [
                    (wbuf[:, 4096 * (p % 2):4096 * (p % 2 + 1)].rearrange("p (n e) -> p n e", e=PAN),
                     w_pan_in[p].rearrange("(n p) e -> p n e", p=128)),
                    (ubuf[:, 512 * (p % 2):512 * (p % 2) + 512].rearrange("p (n e) -> p n e", e=PAN),
                     up_pan_in[p].rearrange("(n p) e -> p n e", p=128)),
                    (x_stg[p % 2].rearrange("p (n e) -> p n e", e=PAN),
                     x_pan_in[p].rearrange("(n p) e -> p n e", p=128)),
                ])


        @block.sync
        def _(s):
            db = [0]
            s.wait_ge(sWS, 16)
            cid = s.snap(s.value_load(meta_sb[0:1, 0:1]), min_val=0, max_val=7)
            partner = s.snap(s.value_load(meta_sb[0:1, 1:2]), min_val=0, max_val=7)

            def dma(out, in_, **kw):
                s.dma_start(out=out, in_=in_, **kw).then_inc(dsem, 16)
                db[0] += 16
                s.wait_ge(dsem, db[0])

            def poll(want, tag):
                with s.register(f"lc_{tag}") as lc, s.register(f"mx_{tag}") as mx, \
                     s.register(f"pt_{tag}") as pt:
                    s.reg_mov(lc, 1)
                    s.reg_mov(mx, MAXIT)
                    s.reg_mov(pt, 0)
                    with s.While(lc):
                        s.dma_start(out=flf[0:1, 0:1],
                                    in_=flags.ap()[0:1, bass.ds(partner, 1)]).then_inc(psem, 16)
                        s.reg_alu(pt, pt, 16, ALU.add)
                        s.wait_ge(psem, pt)
                        v = s.value_load(flf[0:1, 0:1])
                        s.reg_alu(lc, v, want, ALU.subtract)
                        s.reg_alu(mx, mx, 1, ALU.subtract)
                        with s.If_eq(mx, 0):
                            s.reg_mov(lc, 0)

            s.wait_ge(sQas, 2)
            dma(mail.ap()[:, bass.ds(cid, 1)], q_sb[:, 0:1], allow_slow_non_contiguous=True)
            dma(flags.ap()[0:1, bass.ds(cid, 1)], meta_sb[0:1, 2:3])
            poll(MAGIC, "a")
            dma(qex_sb[:], mail.ap()[:, bass.ds(partner, 1)], allow_slow_non_contiguous=True)
            dma(flags.ap()[0:1, bass.ds(cid, 1)], meta_sb[0:1, 3:4])
            poll(ANTI, "b")
            s.sem_inc(sMail, 1)

            s.wait_ge(sQas, 6)
            dma(coeff_dram[0:1, :], coef_sb[0:1, 8:16])
            cfl = coeff_dram.ap().rearrange("one e -> (one e)")
            for t_ in range(4):
                for h in range(2):
                    dma(ccol_sb[64 * h:64 * h + 64, t_:t_ + 1],
                        cfl[2 * t_ + h:2 * t_ + h + 1].unsqueeze(0).broadcast_to([1, 64]),
                        allow_slow_non_contiguous=True)
            s.sem_inc(sCc, 1)

            for p in range(NPAN):
                s.wait_ge(sFu, 8 * (p + 1))
                s.dma_start(out=out_pan[p].rearrange("(n p) e -> p n e", p=128),
                            in_=o_stg[p % 2].rearrange("p (n e) -> p n e", e=PAN)).then_inc(sOut, 16)
            ndbg = 0
            if DBG:
                def ddump(dst, src, wait=None):
                    nonlocal ndbg
                    if wait is not None:
                        s.wait_ge(*wait)
                    s.dma_start(out=dst, in_=src).then_inc(sOut, 16)
                    ndbg += 1
                ddump(dbg_m[:], statp[32:33, :], (sNrm, 64))
                ddump(dbg_r[:], statq[32:33, :])
                ddump(dbg_xn[:], ring[:, xn_slot(0) * TPC:(xn_slot(0) + 1) * TPC])
                ddump(dbg_mv[:], mv_sb[:])
                ddump(dbg_akv[:, 0:TPC], a_sb, (sEvV, N_EVTR))
                ddump(dbg_akv[:, TPC:2 * TPC], k_sb)
                ddump(dbg_akv[:, 2 * TPC:3 * TPC], v_sb)
                ddump(dbg_o[:], o_sb[:], (sAtV, 64))
                ddump(dbg_q[:], q_sb[:], (sQas, 6))
                ddump(dbg_c[:], coef_sb[:])
                ddump(dbg_cc[:], ccol_sb[:], (sCc, 1))
                ddump(dbg_cap1[:], statq[64:65, :])
                ddump(dbg_sc[:], sc_v, (sAtV, 64))
                ddump(dbg_e[:], e_v)
                ddump(dbg_f[:], f_v)
                ddump(dbg_dn[:], regB[:, 4864:4912])
                ddump(dbg_mvb[:], mvb_sb[:])
                ddump(dbg_r31[:], regB[:, 15872:16384].bitcast(BF16))
                ddump(dbg_dc[:], dcap_sb[:])
                ddump(dbg_cap2[:], t2f_all[:, 0:128])
                ddump(dbg_cap3[:], bqs[:])
                ddump(dbg_hsT[:, 0:2 * TPC], wk_sb[:, 0:2 * TPC], (sEvV, N_EVHT))
                ddump(dbg_hsT[:, 2 * TPC:], wv_sb[:, 0:2 * TPC])
            s.wait_ge(sOut, 16 * NPAN + 16 * ndbg)

        @block.tensor
        def _(t):
            for i in range(NIT):
                t.wait_ge(*stX.ready(i // 4))
                t.wait_ge(sSq, i + 1)
                t.matmul(ps[0][0:1, :], onesb_sb[:, 0:1], ring_x(i, 0, 512),
                         start=(i == 0), stop=(i == NIT - 1), skip_group_check=True)
                t.matmul(ps[1][0:1, :], onesb_sb[:, 0:1], ring_x(i, 512, 1024),
                         start=(i == 0), stop=(i == NIT - 1), skip_group_check=True)
                sq = sqb[i % 2]
                t.matmul(ps[2][0:1, :], onesb_sb[:, 0:1], sq[:, 0:512],
                         start=(i == 0), stop=(i == NIT - 1), skip_group_check=True)
                t.matmul(ps[3][0:1, :], onesb_sb[:, 0:1], sq[:, 512:1024],
                         start=(i == 0), stop=(i == NIT - 1), skip_group_check=True
                         ).then_inc(sTst, 1)
            t.wait_ge(sFin, 2)
            t.matmul(ps[2][:, :], ones_sb[32:33, :], statp[32:33, 0:512], start=True, stop=True)
            t.matmul(ps[3][:, :], ones_sb[32:33, :], statp[32:33, 512:1024], start=True, stop=True)
            t.matmul(ps[4][:, :], ones_sb[0:1, :], statp[0:1, 0:512], start=True, stop=True)
            t.matmul(ps[5][:, :], ones_sb[0:1, :], statp[0:1, 512:1024], start=True, stop=True
                     ).then_inc(sBC, 1)
            t.wait_ge(sFin, 1)
            for i in range(NIT):
                t.wait_ge(*stX.ready(8 + i // 4))
                t.wait_ge(sSq, NIT + i + 1)
                t.matmul(ps[0][0:1, :], onesb_sb[:, 0:1], red_t(i)[:, 0:512],
                         start=(i == 0), stop=(i == NIT - 1), skip_group_check=True)
                t.matmul(ps[1][0:1, :], onesb_sb[:, 0:1], red_t(i)[:, 512:1024],
                         start=(i == 0), stop=(i == NIT - 1), skip_group_check=True)
                sq = sqb[i % 2]
                t.matmul(ps[6][0:1, :], onesb_sb[:, 0:1], sq[:, 0:512],
                         start=(i == 0), stop=(i == NIT - 1), skip_group_check=True)
                t.matmul(ps[7][0:1, :], onesb_sb[:, 0:1], sq[:, 512:1024],
                         start=(i == 0), stop=(i == NIT - 1), skip_group_check=True
                         ).then_inc(sTst, 1)
            t.wait_ge(sNrm, NIT)
            t.wait_ge(sFin, 4)
            t.matmul(ps[2][:, :], ones_sb[0:1, :], statp[0:1, 0:512], start=True, stop=True)
            t.matmul(ps[3][:, :], ones_sb[0:1, :], statp[0:1, 512:1024], start=True, stop=True)
            t.wait_ge(sFin, 5)
            t.matmul(ps[4][:, 0:1], ones_sb[0:1, :], t1f[0:1, 0:1], start=True, stop=True
                     ).then_inc(sBC, 1)
            t.wait_ge(sWQr, NIT)
            t.wait_ge(sWS, T_WS)
            t.wait_ge(sFin, 3)
            qg = 0
            for w_sb in (wqsum, wk_sb, wv_sb):
                for c in range(2):
                    if qg >= 2:
                        t.wait_ge(sEvV, qg - 1)
                    for i in range(NIT):
                        mm = t.matmul(ps[6 + c][:, :], w_sb[:, 128 * i:128 * (i + 1)],
                                      ring_xn(i, 512 * c, 512 * (c + 1)),
                                      start=(i == 0), stop=(i == NIT - 1))
                    mm.then_inc(sQKV, 1)
                    qg += 1
            t.wait_ge(sQas, 1)
            for i in range(2 * NIT):
                mm = t.matmul(ps[5][:, 0:1], wqf_sb[:, 128 * (i % NIT):128 * (i % NIT + 1)],
                              mvb_sb[:, i:i + 1],
                              start=(i == 0), stop=(i == 2 * NIT - 1))
            mm.then_inc(sQKV, 1)
            t.wait_ge(sEvV, N_EVQKV)
            t.wait_ge(sFin, 5)
            nb = 0
            for src in (a_sb, k_sb, v_sb):
                for half in range(2):
                    bank = ps[nb % 2]
                    if nb >= 2:
                        t.wait_ge(sEvV, N_EVQKV + nb - 1)
                    for j in range(4):
                        tr = t.transpose(bank[:, 128 * j:128 * (j + 1)],
                                         src[:, 512 * half + 128 * j:512 * half + 128 * (j + 1)],
                                         ident_sb[:])
                    tr.then_inc(sTr, 1)
                    nb += 1
            t.wait_ge(sQas, 3)
            t.matmul(ps[4][0:8, 0:1], keys_sb[:, :], q_sb[:, 1:2], start=True, stop=True
                     ).then_inc(sQKV, 1)
            t.wait_ge(sQas, 4)
            t.transpose(ps[5][0:1, 0:8], q_sb[0:8, 2:3], ident_sb[0:8, 0:8]).then_inc(sQKV, 1)
            t.wait_ge(sQas, 6)
            t.wait_ge(sEvV, N_EVTR)
            t.wait_ge(sNrm, 64)
            for i in range(NIT):
                t.wait_ge(*stDN.ready(i // 4))
                sl = 512 * (i % 8)
                for b in range(4):
                    for c in range(2):
                        mm = t.matmul(ps[2 * b + c][:, :],
                                      dbuf8[:, sl + 128 * b:sl + 128 * (b + 1)],
                                      ring_xn(i, 512 * c, 512 * (c + 1)),
                                      start=(i == 0), stop=(i == NIT - 1), skip_group_check=True)
                mm.then_inc(sHT, 1)
            t.wait_ge(sEvV, N_EVHT)
            fi = 0
            for p in range(NPAN):
                t.wait_ge(*stPN.ready(p))
                for tt in range(8):
                    bank = ps[fi % 4]
                    if fi >= 4:
                        t.wait_ge(sSil, fi - 3)
                    for i in range(NIT):
                        t.matmul(bank[:, 0:PAN],
                                 ring[:, xn_slot(i) * TPC + 128 * tt:xn_slot(i) * TPC + 128 * (tt + 1)],
                                 wbuf[:, 4096 * (p % 2) + 128 * i:4096 * (p % 2) + 128 * (i + 1)],
                                 start=(i == 0), stop=False, skip_group_check=True)
                    for b in range(4):
                        hsrc = wk_sb if b < 2 else wv_sb
                        mm = t.matmul(bank[:, 0:PAN],
                                      hsrc[:, 1024 * (b % 2) + 128 * tt:1024 * (b % 2) + 128 * (tt + 1)],
                                      ubuf[:, 512 * (p % 2) + 128 * b:512 * (p % 2) + 128 * (b + 1)],
                                      start=False, stop=(b == 3), skip_group_check=True)
                    mm.then_inc(sBase, 1)
                    fi += 1

        @block.scalar
        def _(sc):
            for i in range(NIT):
                sc.wait_ge(*stX.ready(i // 4))
                if i >= 2:
                    sc.wait_ge(sTst, i - 1)
                sc.activation(sqb[i % 2][:, :], ring_x(i, 0, TPC), AF.Square).then_inc(sSq, 1)
            sc.wait_ge(sFin, 1)
            sc.activation(t1f[0:1, :], statq[0:1, :], AF.Sqrt).then_inc(sSf, 1)
            for i in range(NIT):
                sc.wait_ge(*stX.ready(8 + i // 4))
                sc.wait_ge(sTst, NIT + i - 1)
                sc.activation(sqb[i % 2][:, :], red_t(i), AF.Square).then_inc(sSq, 1)
            sc.wait_ge(sFin, 3)
            sc.activation(t1f[0:1, :], statq[0:1, :], AF.Sqrt).then_inc(sSf, 1)
            sc.wait_ge(sQas, 5)
            sc.activation(coef_sb[0:1, 0:8], ps[5][0:1, 0:8], AF.Exp,
                          bias=coef_sb[0:1, 9:10]).then_inc(sSf, 1)
            fi = 0
            for q in range(8):
                for j in range(8):
                    it = 8 * q + j
                    sc.wait_ge(sScV, it + 1)
                    if it >= 1:
                        sc.wait_ge(sAtV, it)
                    sc.activation(e_v, sc_v, AF.Exp).then_inc(sAtS, 1)
                if q >= 1:
                    for _ in range(32):
                        sc.wait_ge(sBase, fi + 1)
                        if fi >= 2:
                            sc.wait_ge(sG, fi - 1)
                        sc.activation(g_sb[:, PAN * (fi % 2):PAN * (fi % 2 + 1)],
                                      ps[fi % 4][:, 0:PAN], AF.Silu).then_inc(sSil, 1)
                        fi += 1
            while fi < 8 * NPAN:
                sc.wait_ge(sBase, fi + 1)
                if fi >= 2:
                    sc.wait_ge(sG, fi - 1)
                sc.activation(g_sb[:, PAN * (fi % 2):PAN * (fi % 2 + 1)],
                              ps[fi % 4][:, 0:PAN], AF.Silu).then_inc(sSil, 1)
                fi += 1

        @block.vector
        def _(v):
            v.wait_ge(sWS, T_WS)
            v.reduce_sum(bqs[:, 0:1], bq_sb[:, :], axis=mybir.AxisListType.X)

            v.wait_ge(sTst, NIT)
            v.tensor_scalar(statp[0:1, 0:512], ps[0][0:1, :], 1.0 / 4096.0, None, ALU.mult)
            v.tensor_scalar(statp[0:1, 512:1024], ps[1][0:1, :], 1.0 / 4096.0, None, ALU.mult)
            v.tensor_copy(statp[32:33, :], statp[0:1, :])
            v.tensor_scalar(statq[0:1, 0:512], ps[2][0:1, :], 1.0 / 4096.0, None, ALU.mult)
            v.tensor_scalar(statq[0:1, 512:1024], ps[3][0:1, :], 1.0 / 4096.0, None, ALU.mult)
            v.tensor_tensor(t1f[0:1, :], statp[0:1, :], statp[0:1, :], ALU.mult)
            v.tensor_tensor(statq[0:1, :], statq[0:1, :], t1f[0:1, :], ALU.subtract)
            v.tensor_scalar(statq[0:1, :], statq[0:1, :], EPS, None, ALU.add
                            ).then_inc(sFin, 1)
            v.wait_ge(sSf, 1)
            v.reciprocal(statp[0:1, :], t1f[0:1, :]).then_inc(sFin, 1)
            v.tensor_copy(statq[32:33, :], statp[0:1, :])
            v.wait_ge(sBC, 1)
            for i in range(NIT):
                for c in range(2):
                    v.tensor_tensor(t1f[:, 512 * c:512 * (c + 1)], ring_x(i, 512 * c, 512 * (c + 1)),
                                    ps[2 + c][:, :], ALU.subtract)
                for c in range(2):
                    st_ = v.scalar_tensor_tensor(
                        ring_xn(i, 512 * c, 512 * (c + 1)), t1f[:, 512 * c:512 * (c + 1)],
                        1.0, ps[4 + c][:, :], ALU.mult, ALU.mult,
                        accum_out=o_sb[:, 2 * i + c:2 * i + c + 1])
                st_.then_inc(sNrm, 1)
            v.tensor_tensor(t1f[0:1, 0:512], statp[0:1, 0:512], statp[0:1, 0:512], ALU.add)
            for i in range(NIT):
                v.tensor_tensor(mv_sb[:, i:i + 1], o_sb[:, 2 * i:2 * i + 1],
                                o_sb[:, 2 * i + 1:2 * i + 2], ALU.add)
            for i in range(NIT):
                v.wait_ge(*stWQ.ready(i))
                stg = wqstg[:, 4096 * (i % 2):4096 * (i % 2 + 1)]
                s3 = stg.rearrange("p (d h) -> p d h", h=32)
                l1 = wqscr[:, 0:2048].rearrange("p (d h) -> p d h", h=16)
                v.tensor_tensor(l1, s3[:, :, 0:16], s3[:, :, 16:32], ALU.add)
                with nc.allow_low_precision(reason="bf16 weight-sum: operand-level noise"):
                    v.tensor_reduce(wqsum[:, 128 * i:128 * (i + 1)].unsqueeze(2), l1,
                                    axis=mybir.AxisListType.X, op=ALU.add).then_inc(sWQr, 1)
            v.wait_ge(sTst, 2 * NIT)
            v.wait_ge(sBC, 1)
            v.tensor_scalar(statp[0:1, 0:512], ps[0][0:1, :], 1.0 / 4096.0, None, ALU.mult)
            v.tensor_scalar(statp[0:1, 512:1024], ps[1][0:1, :], 1.0 / 4096.0, None, ALU.mult)
            v.tensor_copy(ps[0][0:1, :], statp[0:1, 0:512])
            v.tensor_copy(ps[1][0:1, :], statp[0:1, 512:1024])
            v.tensor_scalar(statq[0:1, 0:512], ps[6][0:1, :], 1.0 / 4096.0, None, ALU.mult)
            v.tensor_scalar(statq[0:1, 512:1024], ps[7][0:1, :], 1.0 / 4096.0, None, ALU.mult)
            v.tensor_tensor(t1f[0:1, :], statp[0:1, :], statp[0:1, :], ALU.mult)
            v.tensor_tensor(statq[0:1, :], statq[0:1, :], t1f[0:1, :], ALU.subtract)
            v.tensor_scalar(statq[0:1, :], statq[0:1, :], EPS, None, ALU.add
                            ).then_inc(sFin, 1)
            v.wait_ge(sSf, 2)
            v.reciprocal(statp[0:1, :], t1f[0:1, :]).then_inc(sFin, 1)
            v.tensor_tensor(t1f[0:1, 0:512], ps[0][0:1, :], statp[0:1, 0:512], ALU.mult)
            v.tensor_tensor(t1f[0:1, 512:1024], ps[1][0:1, :], statp[0:1, 512:1024], ALU.mult)
            v.reduce_sum(t1f[0:1, 0:1], t1f[0:1, :], axis=mybir.AxisListType.X
                         ).then_inc(sFin, 1)
            v.wait_ge(sBC, 2)
            for r in range(NIT):
                for c in range(2):
                    st_ = v.scalar_tensor_tensor(
                        t1f[:, 512 * c:512 * (c + 1)], red_t(r)[:, 512 * c:512 * (c + 1)],
                        1.0, ps[2 + c][:, :], ALU.mult, ALU.mult,
                        accum_out=o_sb[:, 64 + 2 * r + c:64 + 2 * r + c + 1])
                st_.then_inc(sNrm, 1)
            v.tensor_tensor(t1f[0:1, 0:512], statp[0:1, 0:512], statp[0:1, 0:512], ALU.add)
            for r in range(NIT):
                v.tensor_tensor(o_sb[:, 128 + r:129 + r], o_sb[:, 64 + 2 * r:64 + 2 * r + 1],
                                o_sb[:, 64 + 2 * r + 1:64 + 2 * r + 2], ALU.add)
            for r in range(NIT):
                v.tensor_tensor(mv_sb[:, 32 + r:33 + r], o_sb[:, 128 + r:129 + r],
                                ps[4][:, 0:1], ALU.subtract)
            if DBG:
                v.tensor_copy(dcap_sb[:, 0:2], ps[2][:, 0:2])
                v.tensor_copy(dcap_sb[:, 2:4], ps[3][:, 0:2])
                v.tensor_copy(dcap_sb[:, 4:6], ps[2][:, 200:202])
                v.tensor_copy(statq[64:65, 0:512], ps[2][0:1, :])
                v.tensor_copy(statq[64:65, 512:1024], ps[3][0:1, :])
                v.tensor_copy(bqs[:, 3:4], ps[4][:, 0:1])
            v.tensor_tensor(t1f[0:1, 0:512], statp[0:1, 0:512], statp[0:1, 0:512], ALU.add)
            v.tensor_tensor(t1f[0:1, 0:512], statp[0:1, 0:512], statp[0:1, 0:512], ALU.add)
            v.tensor_copy(mvb_sb[:, 0:64], mv_sb[:, 0:64]).then_inc(sQas, 1)
            biases = [bqs[:, 0:1], bqs[:, 0:1], bqs[:, 1:2], bqs[:, 1:2], bqs[:, 2:3], bqs[:, 2:3]]
            dsts = [a_sb, a_sb, k_sb, k_sb, v_sb, v_sb]
            for gidx in range(6):
                c = gidx % 2
                v.wait_ge(sQKV, gidx + 1)
                v.tensor_scalar(dsts[gidx][:, 512 * c:512 * (c + 1)], ps[6 + c][:, :],
                                biases[gidx], None, ALU.add).then_inc(sEvV, 1)
            nb = 0
            for dst in (a_sb, k_sb, v_sb):
                for half in range(2):
                    v.wait_ge(sTr, nb + 1)
                    v.tensor_copy(dst[:, 512 * half:512 * (half + 1)], ps[nb % 2][:, :]
                                  ).then_inc(sEvV, 1)
                    nb += 1
            v.tensor_copy(v_bf, v_sb)

            v.wait_ge(sQKV, 7)
            v.tensor_scalar(q_sb[:, 0:1], ps[5][:, 0:1], Q_SCALE, None, ALU.mult).then_inc(sQas, 1)
            v.wait_ge(sMail, 1)
            v.tensor_tensor(q_sb[:, 1:2], q_sb[:, 0:1], qex_sb[:], ALU.add).then_inc(sQas, 1)
            v.wait_ge(sQKV, 8)
            v.tensor_copy(q_sb[0:8, 2:3], ps[4][0:8, 0:1]).then_inc(sQas, 1)
            v.wait_ge(sQKV, 9)
            v.reduce_max(coef_sb[0:1, 9:10], ps[5][0:1, 0:8], axis=mybir.AxisListType.X,
                         negate=True).then_inc(sQas, 1)
            v.wait_ge(sSf, 3)
            v.reduce_sum(coef_sb[0:1, 10:11], coef_sb[0:1, 0:8], axis=mybir.AxisListType.X)
            v.tensor_tensor(t1f[0:1, 0:512], statp[0:1, 0:512], statp[0:1, 0:512], ALU.add)
            v.tensor_tensor(t1f[0:1, 0:512], statp[0:1, 0:512], statp[0:1, 0:512], ALU.add)
            v.reciprocal(coef_sb[0:1, 11:12], coef_sb[0:1, 10:11])
            v.tensor_tensor(t1f[0:1, 0:512], statp[0:1, 0:512], statp[0:1, 0:512], ALU.add)
            v.tensor_tensor(t1f[0:1, 0:512], statp[0:1, 0:512], statp[0:1, 0:512], ALU.add)
            v.tensor_scalar(coef_sb[0:1, 8:16], coef_sb[0:1, 0:8], coef_sb[0:1, 11:12],
                            None, ALU.mult).then_inc(sQas, 1)
            def attn_iter(q, j):
                it = 8 * q + j
                if it >= 1:
                    v.wait_ge(sAtS, it)
                sc3 = sc_v.rearrange("p (d e) -> p d e", e=128)
                kbc = k_sb[:, 128 * j:128 * (j + 1)].unsqueeze(1).broadcast_to([128, 16, 128])
                abc = a_sb[:, 128 * j + 16 * q:128 * j + 16 * q + 16].unsqueeze(2).broadcast_to([128, 16, 128])
                v.scalar_tensor_tensor(sc3, kbc, ATT_SCALE, abc, ALU.mult, ALU.mult
                                       ).then_inc(sScV, 1)
                v.wait_ge(sAtS, it + 1)
                e3 = e_v.rearrange("p (d e) -> p d e", e=128)
                f3 = f_v.rearrange("p (d e) -> p d e", e=128)
                vbc = v_bf[:, 128 * j:128 * (j + 1)].unsqueeze(1).broadcast_to([128, 16, 128])
                v.tensor_tensor(f3, e3, vbc, ALU.mult)
                tA = trA[:, 0:1024].rearrange("p (d e) -> p d e", e=64)
                tB = trB[:, 0:1024].rearrange("p (d e) -> p d e", e=64)
                tC = trC[:, 0:512].rearrange("p (d e) -> p d e", e=32)
                tD = trD[:, 0:512].rearrange("p (d e) -> p d e", e=32)
                v.tensor_tensor(tA, e3[:, :, 0:64], e3[:, :, 64:128], ALU.add)
                v.tensor_tensor(tB, f3[:, :, 0:64], f3[:, :, 64:128], ALU.add)
                v.tensor_tensor(tC, tA[:, :, 0:32], tA[:, :, 32:64], ALU.add)
                v.tensor_tensor(tD, tB[:, :, 0:32], tB[:, :, 32:64], ALU.add)
                with nc.allow_low_precision(reason="bf16 softmax sums: within operand noise"):
                    v.tensor_reduce(o_den[:, 128 * q + 16 * j:128 * q + 16 * (j + 1)].unsqueeze(2),
                                    tC, axis=mybir.AxisListType.X, op=ALU.add)
                    v.tensor_reduce(o_num[:, 128 * q + 16 * j:128 * q + 16 * (j + 1)].unsqueeze(2),
                                    tD, axis=mybir.AxisListType.X, op=ALU.add).then_inc(sAtV, 1)

            rec_buf = regB[:, 15616:15744]

            def o_div_a(q):
                v.reciprocal(rec_buf, o_den[:, 128 * q:128 * (q + 1)])

            def o_div_b(q):
                dst = o_sb[:].rearrange("p (j d) -> p j d", d=128)[:, :, 16 * q:16 * (q + 1)]
                nsrc = o_num[:, 128 * q:128 * (q + 1)].rearrange("p (j d) -> p j d", d=16)
                rsrc = rec_buf.rearrange("p (j d) -> p j d", d=16)
                v.tensor_tensor(dst, nsrc, rsrc, ALU.mult)

            def fusion_panel(p):
                fbase = 8 * p
                for tt in range(8):
                    fi = fbase + tt
                    v.wait_ge(sSil, fi + 1)
                    g3 = g_sb[:, PAN * (fi % 2):PAN * (fi % 2 + 1)].rearrange("p (d h) -> p d h", h=32)
                    sbc = o_sb[:, 128 * tt + 4 * p:128 * tt + 4 * p + 4].unsqueeze(2).broadcast_to([128, 4, 32])
                    v.tensor_tensor(t2f_all[:, 128 * tt:128 * (tt + 1)].rearrange("p (d h) -> p d h", h=32),
                                    g3, sbc, ALU.mult).then_inc(sG, 1)
                v.wait_ge(*stPN.ready(p))
                if p >= 2:
                    v.wait_ge(sOut, 16 * p)
                for tt in range(8):
                    v.tensor_tensor(o_stg[p % 2][:, 128 * tt:128 * (tt + 1)],
                                    t2f_all[:, 128 * tt:128 * (tt + 1)],
                                    x_stg[p % 2][:, 128 * tt:128 * (tt + 1)], ALU.add
                                    ).then_inc(sFu, 1)

            for j in range(8):
                attn_iter(0, j)
            v.wait_ge(sHT, NIT)
            v.wait_ge(sCc, 1)
            for b in range(4):
                hdst = wk_sb if b < 2 else wv_sb
                for c in range(2):
                    v.tensor_scalar(hdst[:, 1024 * (b % 2) + 512 * c:1024 * (b % 2) + 512 * (c + 1)],
                                    ps[2 * b + c][:, :], ccol_sb[:, b:b + 1], None, ALU.mult
                                    ).then_inc(sEvV, 1)
            for q in range(1, 8):
                attn_iter(q, 0)
                o_div_a(q - 1)
                attn_iter(q, 1)
                o_div_b(q - 1)
                for j in range(2, 8):
                    attn_iter(q, j)
                for p in range(4 * (q - 1), 4 * q):
                    fusion_panel(p)
            o_div_a(7)
            v.tensor_tensor(t1f[0:1, 0:512], statp[0:1, 0:512], statp[0:1, 0:512], ALU.add)
            v.tensor_tensor(t1f[0:1, 0:512], statp[0:1, 0:512], statp[0:1, 0:512], ALU.add)
            o_div_b(7)
            for p in range(28, 32):
                fusion_panel(p)

    return nc


def _prep_inputs(L0_attn_x, L0_attn_Wq, L0_attn_bq, L0_attn_Wk, L0_attn_bk,
                 L0_attn_Wv, L0_attn_bv, L0_ffn_W_org, L0_ffn_lora_down,
                 L0_ffn_lora_up, L0_ffn_lora_keys, L0_ffn_Wq):
    f = np.float32
    x = np.asarray(L0_attn_x, f)
    w_pan = np.ascontiguousarray(
        np.asarray(L0_ffn_W_org, f).T.reshape(D, NPAN, PAN).transpose(1, 0, 2))
    down_r = np.ascontiguousarray(np.asarray(L0_ffn_lora_down, f).transpose(1, 0, 2).reshape(D, 512))
    up_pan = np.ascontiguousarray(
        np.asarray(L0_ffn_lora_up, f).reshape(512, NPAN, PAN).transpose(1, 0, 2))
    wq_r = np.ascontiguousarray(np.asarray(L0_attn_Wq, f).reshape(D, 4096))
    wk2 = np.ascontiguousarray(np.asarray(L0_attn_Wk, f)[:, :, 0])
    wv2 = np.ascontiguousarray(np.asarray(L0_attn_Wv, f)[:, :, 0])
    wqf_T = np.ascontiguousarray(np.asarray(L0_ffn_Wq, f).T)
    keys_T = np.ascontiguousarray(np.asarray(L0_ffn_lora_keys, f).T)
    bq = np.ascontiguousarray(np.asarray(L0_attn_bq, f))
    bk = np.ascontiguousarray(np.asarray(L0_attn_bk, f))
    bv = np.ascontiguousarray(np.asarray(L0_attn_bv, f))
    ident = np.eye(128, dtype=f)
    ones = np.ones((128, 128), dtype=f)
    onesb = np.ones((128, 1), ml_dtypes.bfloat16)

    xT = {}
    xpan = {}
    for b in range(2):
        for blk in range(4):
            sl = x[b, blk * TPC:(blk + 1) * TPC]
            xT[(b, blk)] = np.ascontiguousarray(sl.T)
            xpan[(b, blk)] = np.ascontiguousarray(sl.reshape(TPC, NPAN, PAN).transpose(1, 0, 2))

    in_maps = []
    for c in range(NCORES):
        b, blk = c // 4, c % 4
        meta = np.array([[c, c ^ 1, MAGIC, ANTI, 0, 0, 0, 0]], np.int32)
        in_maps.append({
            "xT_own": xT[(b, blk)],
            "xT_red": xT[(b, (blk + 2) % 4)],
            "x_pan_in": xpan[(b, blk)],
            "wq_in": wq_r, "wk_in": wk2, "wv_in": wv2,
            "bq_in": bq, "bk_in": bk, "bv_in": bv,
            "w_pan_in": w_pan, "down_in": down_r, "up_pan_in": up_pan,
            "wqf_in": wqf_T, "keys_in": keys_T,
            "ident_in": ident, "ones_in": ones, "onesb_in": onesb, "meta_in": meta,
        })
    return in_maps


def kernel(**inputs):
    in_maps = _prep_inputs(**inputs)
    nc = _build()
    res = run_bass_kernel_spmd(nc, in_maps, core_ids=list(range(NCORES)))
    out = np.empty((2, 4096, D), np.float32)
    for c in range(NCORES):
        b, blk = c // 4, c % 4
        op = np.asarray(res.results[c]["out_pan"])
        out[b, blk * TPC:(blk + 1) * TPC] = op.transpose(1, 0, 2).reshape(TPC, D)
    return out
